# version 02 (final)
# baseline (speedup 1.0000x reference)
import math
from contextlib import ExitStack

import numpy as np

import concourse.bass as bass
import concourse.mybir as mybir
from concourse.bass_utils import run_bass_kernel_spmd

F32 = mybir.dt.float32
BF16 = mybir.dt.bfloat16
I32 = mybir.dt.int32
ALU = mybir.AluOpType
ACT = mybir.ActivationFunctionType
AXX = mybir.AxisListType.X

D = 2048
B = 4
S = 2048
TC = 1024
NH = 16
HD = 128
FF = 8192
TT = 256
NTT = TC // TT
NCH = D // 128
NFH = FF // 128
EPS_LN = 1e-5
EPS_Q = 1e-5
C_FWHT = float(np.float32(1.0) / np.sqrt(np.float32(FF)))
INV_SQRT_HD = 1.0 / math.sqrt(HD)
LN2 = math.log(2.0)
TINY = float(np.finfo(np.float32).tiny)

SCHED = [16, 15, 14, 13, 8, 7, 6, 5]
MASKN = 5
RG = [[0, 1], [2, 3], [4, 5], [6, 7]]

ALL_PHASES = ("p1", "p2", "p3", "p4", "p5", "p6")


def tok_global(h, l):
    if h == 0:
        return l if l < 512 else 1024 + l
    return 512 + l


def qtile_global(h, lt):
    return tok_global(h, lt * 128) // 128


def _hadamard(n):
    hm = np.array([[1.0]])
    while hm.shape[0] < n:
        hm = np.block([[hm, hm], [hm, -hm]])
    return hm


def host_constants(h):
    import ml_dtypes
    c = {}
    c["h128"] = _hadamard(128).astype(ml_dtypes.bfloat16)
    c["idn"] = np.eye(128, dtype=np.float32)
    e = np.zeros(TC, dtype=np.float32)
    if h == 0:
        e[:512] = 1e9
        e[512:] = np.arange(1, 513, dtype=np.float32)
    else:
        e[:] = np.arange(1, TC + 1, dtype=np.float32)
    c["scanE"] = np.ascontiguousarray(np.tile(e, NH).reshape(NH, TC))
    fl = np.array([0, 1, 0, 1] if h == 0 else [0, 1, 1, 1], dtype=np.float32)
    c["scanflag"] = np.repeat(fl[None, :], 128, axis=0).copy()
    c["snapf"] = np.repeat(np.array([[1.0, 0.0] if h == 0 else [0.0, 1.0]],
                                    dtype=np.float32), 128, axis=0)
    c["cflags"] = np.repeat(np.array([[0.0, 1.0, 1.0] if h == 0 else [1.0, 0.0, 0.0]],
                                     dtype=np.float32), 128, axis=0)
    masks = np.zeros((8, 128, MASKN * 128), dtype=np.float32)
    for sl in range(8):
        gq = qtile_global(h, 7 - sl)
        causal = gq + 1
        for j in range(MASKN):
            kk = SCHED[sl] - MASKN + j
            if kk < causal - 1:
                masks[sl, :, j * 128:(j + 1) * 128] = 1.0
            elif kk == causal - 1:
                masks[sl, :, j * 128:(j + 1) * 128] = (
                    np.arange(128)[:, None] <= np.arange(128)[None, :]).astype(np.float32)
    c["masks"] = masks.reshape(8 * 128, MASKN * 128).copy()
    toks = np.array([tok_global(h, l) for l in range(TC)], dtype=np.float32)
    rot_dim = D // 2
    inv_freq = (1.0 / (10000.0 ** (np.arange(0, rot_dim, 2, dtype=np.float32) / rot_dim))).astype(np.float32)
    fr = np.repeat(toks[:, None] * inv_freq[None, :], 2, axis=-1).astype(np.float32)
    cos = np.cos(fr).astype(np.float32)
    sin = np.sin(fr).astype(np.float32)
    c["cos_e"] = np.ascontiguousarray(cos[:, 0::2])
    c["cos_o"] = np.ascontiguousarray(cos[:, 1::2])
    c["sin_e"] = np.ascontiguousarray(sin[:, 0::2])
    c["sin_o"] = np.ascontiguousarray(sin[:, 1::2])
    return c


ENGS = ("sync", "vector", "scalar", "tensor", "gpsimd")


class Prog:
    fence_scalar = None

    def __init__(self, nc, es):
        self.nc = nc
        self.es = es
        self.q = {e: [] for e in ENGS}
        self.cnt = {}
        self.last_wait = {e: {} for e in ENGS}
        self.sems = {}
        self.bank_w = [None] * 8
        self.bank_r = [[] for _ in range(8)]
        self.dma_sems = {e: set() for e in ENGS}

    def sem(self, name):
        if name not in self.sems:
            self.sems[name] = self.es.enter_context(self.nc.semaphore(name))
            self.cnt[name] = 0
        return self.sems[name]

    def wait(self, eng, token):
        if token is None:
            return
        name, val = token
        if val <= self.last_wait[eng].get(name, 0):
            return
        self.last_wait[eng][name] = val
        self.q[eng].append(("w", name, val))

    def op(self, eng, fn, waits=(), inc=None, n=1):
        merged = {}
        for t in waits:
            if t is None:
                continue
            merged[t[0]] = max(merged.get(t[0], 0), t[1])
        for s, v in merged.items():
            self.wait(eng, (s, v))
        if inc is None:
            inc = {"vector": "sv", "scalar": "ss", "tensor": "st",
                   "gpsimd": "sg", "sync": "sy"}[eng]
        self.sem(inc)
        self.cnt[inc] += n
        tok = (inc, self.cnt[inc])
        self.q[eng].append(("i", fn, inc, n))
        return tok

    def dma(self, eng, out, in_, waits=(), sem=None):
        name = sem or ("dma_" + eng)
        self.dma_sems[eng].add(name)
        return self.op(eng, lambda e, o=out, i=in_: e.dma_start(out=o, in_=i),
                       waits=waits, inc=name, n=16)

    def barrier(self):
        for e in ("sync", "gpsimd", "scalar"):
            for qn in sorted(self.dma_sems[e]):
                if self.cnt[qn] > self.last_wait[e].get(qn, 0):
                    self.wait(e, (qn, self.cnt[qn]))
        name = "bar"
        self.sem(name)
        for e in ENGS:
            if e == "vector" or e == "gpsimd":
                self.op(e, lambda eng: eng.engine_nop(), inc=name)
            elif e == "scalar":
                self.op(e, self.fence_scalar, inc=name)
            else:
                self.op(e, lambda eng: eng.nop(), inc=name)
        final = (name, self.cnt[name])
        for e in ENGS:
            self.wait(e, final)
        for i in range(8):
            self.bank_w[i] = None
            self.bank_r[i] = []

    def check(self):
        pos = {e: 0 for e in ENGS}
        val = {n: 0 for n in self.sems}
        progressed = True
        while progressed:
            progressed = False
            for e in ENGS:
                while pos[e] < len(self.q[e]):
                    it = self.q[e][pos[e]]
                    if it[0] == "w":
                        if val[it[1]] < it[2]:
                            break
                        pos[e] += 1
                    else:
                        val[it[2]] += it[3]
                        pos[e] += 1
                    progressed = True
        stuck = {}
        for e in ENGS:
            if pos[e] < len(self.q[e]):
                it = self.q[e][pos[e]]
                stuck[e] = (f"{pos[e]}/{len(self.q[e])} " +
                            (f"waiting {it[1]}>={it[2]} (now {val[it[1]]})"
                             if it[0] == "w" else "?"))
        if stuck:
            raise RuntimeError(f"program deadlock: {stuck}")

    def replay(self):
        self.check()
        sems = self.sems

        def run(eng_h, items):
            for it in items:
                if it[0] == "w":
                    eng_h.wait_ge(sems[it[1]], it[2])
                else:
                    it[1](eng_h).then_inc(sems[it[2]], it[3])

        with self.nc.Block() as block:
            @block.sync
            def _(e):
                run(e, self.q["sync"])

            @block.vector
            def _(e):
                run(e, self.q["vector"])

            @block.scalar
            def _(e):
                run(e, self.q["scalar"])

            @block.tensor
            def _(e):
                run(e, self.q["tensor"])

            @block.gpsimd
            def _(e):
                run(e, self.q["gpsimd"])


P2_STOP = "full"


def build_program(phases=ALL_PHASES):
    nc = bass.Bass("TRN2", target_bir_lowering=False)
    es = ExitStack()
    p = Prog(nc, es)

    def kpj0(ap, pp=128):
        return ap.rearrange("(k p) j -> k p j", p=pp).rearrange("k p j -> p k j")

    def dram_in(name, shape, dt=F32):
        return nc.dram_tensor(name, list(shape), dt, kind="ExternalInput").ap()

    def dram_int(name, shape, dt=F32):
        return nc.dram_tensor(name, list(shape), dt).ap()

    def sbuf(name, shape, dt=F32):
        return es.enter_context(nc.sbuf_tensor(name, list(shape), dt)).ap()

    x_in = dram_in("x_in", [D, TC])
    x_out = nc.dram_tensor("x_out", [D, TC], F32, kind="ExternalOutput").ap()
    wqT = dram_in("wqT", [D, D]); wvT = dram_in("wvT", [D, D])
    wgT = dram_in("wgT", [D, D]); woT = dram_in("woT", [D, D])
    l0g_d = dram_in("l0g", [128, NCH]); l0b_d = dram_in("l0b", [128, NCH])
    beta_xd = dram_in("beta_x", [128, NH]); beta_rd = dram_in("beta_row", [1, NH])
    wgateT = dram_in("wgateT", [D, NH])
    wqkvT = dram_in("wqkvT", [D, 3 * D])
    winP = dram_in("winP", [D, 3 * D])
    woutP = dram_in("woutP", [D, D])
    b_inP = dram_in("b_inP", [1, 3 * D])
    b_out_d = dram_in("b_out_c", [128, NCH])
    l1g_d = dram_in("l1g", [128, NCH]); l1b_d = dram_in("l1b", [128, NCH])
    wgf = dram_in("wgf", [D, FF]); wuf = dram_in("wuf", [D, FF])
    wdf = dram_in("wdf", [FF, D])
    h128_d = dram_in("h128", [128, 128], BF16)
    idn_d = dram_in("idn", [128, 128])
    scanE_d = dram_in("scanE", [NH, TC])
    scanflag_d = dram_in("scanflag", [128, NTT])
    snapf_d = dram_in("snapf", [128, 2])
    cflags_d = dram_in("cflags", [128, 3])
    masks_d = dram_in("masks", [8 * 128, MASKN * 128])
    cos_ed = dram_in("cos_e", [TC, 512]); cos_od = dram_in("cos_o", [TC, 512])
    sin_ed = dram_in("sin_e", [TC, 512]); sin_od = dram_in("sin_o", [TC, 512])

    terng = dram_int("terng", [NFH, NCH * 128 * 128], BF16)
    ternu = dram_int("ternu", [NFH, NCH * 128 * 128], BF16)
    ternd = dram_int("ternd", [NCH, NFH * 128 * 128], BF16)
    drow_d = dram_int("drow_d", [NH, TC])
    ccs_in = dram_int("ccs_in", [128, NH]); ccs_out = dram_int("ccs_out", [2, D])
    cckv_in = dram_int("cckv_in", [2 * TC, 128])
    cckv_out = dram_int("cckv_out", [2, 2 * TC * 128])
    cca_in = dram_int("cca_in", [128, NH]); cca_out = dram_int("cca_out", [2, D])

    xres = sbuf("xres", [128, NCH * TC])
    RA = sbuf("RA", [128, 16384])
    RC = sbuf("RC", [128, 10240])
    t1 = sbuf("t1", [128, 1024]); t2 = sbuf("t2", [128, 1024])
    tint = es.enter_context(nc.sbuf_tensor("tint", [128, 1024], I32)).ap()
    hil = sbuf("hil", [128, TT], BF16); lol = sbuf("lol", [128, TT], BF16)
    h128 = sbuf("h128s", [128, 128], BF16)
    idn = sbuf("idns", [128, 128])
    onec = sbuf("onec", [128, 1]); onerow = sbuf("onerow", [1, 128])
    rowbuf = sbuf("rowbuf", [1, 1024])
    rowb2 = sbuf("rowb2", [1, 1024])
    mur = sbuf("mur", [1, TT]); sqr = sbuf("sqr", [1, TT]); rsr = sbuf("rsr", [1, TT])
    r1 = sbuf("r1", [1, TT]); r2 = sbuf("r2", [1, TT]); r3 = sbuf("r3", [1, TT])
    r4 = sbuf("r4", [1, TT]); r5 = sbuf("r5", [1, TT])
    lamx = sbuf("lamx", [128, NH]); lnlr = sbuf("lnlr", [1, NH])
    lam1024 = sbuf("lam1024", [1, NH])
    betap = sbuf("betap", [128, NH]); betar = sbuf("betar", [1, NH])
    l0gs = sbuf("l0gs", [128, NCH]); l0bs = sbuf("l0bs", [128, NCH])
    l1gs = sbuf("l1gs", [128, NCH]); l1bs = sbuf("l1bs", [128, NCH])
    bouts = sbuf("bouts", [128, NCH])
    sflag = sbuf("sflag", [128, NTT])
    snapf = sbuf("snapf_s", [128, 2]); cflag = sbuf("cflag", [128, 3])
    cfl1 = cflag[:, 0:1]; cfl3 = cflag[:, 1:2]; cflb = cflag[:, 2:3]
    wgt16 = sbuf("wgt16", [128, 256])
    agt0 = sbuf("agt0", [128, NH]); agt1 = sbuf("agt1", [128, NH])
    scanstate = sbuf("scanstate", [128, NH])
    gzero = sbuf("gzero", [128, NH])
    gtok = sbuf("gtok", [128, NH])
    gind = sbuf("gind", [128, 8 * NH])
    gval = sbuf("gval", [128, 8])
    scw = sbuf("scw", [1, 4])
    wscol = sbuf("wscol", [128, 1]); accv = sbuf("accv", [128, 1])
    sc1 = sbuf("sc1", [1, 1]); mnkr = sbuf("mnkr", [1, 1])
    acol = sbuf("acol", [128, NCH])
    bc6 = sbuf("bc6", [128, TT]); bc7 = sbuf("bc7", [128, TT])

    pb = [es.enter_context(nc.psum_tensor(f"pb{i}", [128, 512], F32)).ap()
          for i in range(8)]
    fencet = sbuf("fencet", [1, 1])
    p.fence_scalar = (lambda e: e.activation(fencet[:, :], fencet[:, :],
                                             ACT.Identity))

    TEN, VEC, SCA, SYN, GPS = "tensor", "vector", "scalar", "sync", "gpsimd"

    slab = RA
    xq2 = RA[:, 0:8192].bitcast(BF16)
    retv = RA
    xlnA = RA
    xcen = RC[:, 0:4096]
    xqs = RC[:, 0:2048].bitcast(BF16)
    wfa0 = RC[:, 0:2048]; wfb0 = RC[:, 2048:4096]
    wba = RC[:, 4096:5120].bitcast(BF16); wbb = RC[:, 5120:6144].bitcast(BF16)
    wga = RC[:, 4096:5120].bitcast(BF16); wgb = RC[:, 5120:6144].bitcast(BF16)
    wua = RC[:, 6144:7168].bitcast(BF16); wub = RC[:, 7168:8192].bitcast(BF16)
    wda = RC[:, 0:4096].bitcast(BF16); wdb = RC[:, 4096:8192].bitcast(BF16)
    wfa = RC[:, 4096:6144]; wfb = RC[:, 6144:8192]
    wkva = RC[:, 0:4096]; wkvb = RC[:, 4096:8192]
    kown = RC[:, 8192:9216]; vown = RC[:, 9216:10240]
    kfm = RC[:, 0:2048]
    vshg = RC[:, 2048:4096]
    qsl = RC[:, 8192:8448]
    pslb = RC[:, 8448:8704]
    masksb = RC[:, 8704:9344]
    ktok = RC[:, 0:2048]; vtok = RC[:, 2048:4096]
    wkv5a = RC[:, 4096:6144]; wkv5b = RC[:, 6144:8192]
    cs_e = RC[:, 8192:8704]; cs_o = RC[:, 8704:9216]
    sn_e = RC[:, 9216:9728]; sn_o = RC[:, 9728:10240]


    def kpj(ap, pp=128):
        return ap.rearrange("(k p) j -> k p j", p=pp).rearrange("k p j -> p k j")

    def mm(psum, lhsT, rhs, start, stop, waits=()):
        return p.op(TEN, lambda e, o=psum, l=lhsT, r=rhs, s1=start, s2=stop:
                    e.matmul(o, l, r, start=s1, stop=s2), waits=waits)

    def tr(psum, in_, waits=()):
        k = in_.shape[0]
        return p.op(TEN, lambda e, o=psum, i=in_, k_=k: e.transpose(o, i, idn[0:k_, 0:k_]),
                    waits=waits)

    def vo(fn, waits=()):
        return p.op(VEC, fn, waits=waits)

    def so(fn, waits=()):
        return p.op(SCA, fn, waits=waits)

    def bank_start(i, waits=()):
        w = list(waits) + [t for t in p.bank_r[i] if t] + ([p.bank_w[i]] if p.bank_w[i] else [])
        p.bank_r[i] = []
        return tuple(w)

    def bank_written(i, tok):
        p.bank_w[i] = tok

    def bank_read(i, tok):
        p.bank_r[i].append(tok)

    def emit_round(dst, src, waits=()):
        n = src.shape[1]
        a = t2[:, 0:n]
        t = vo(lambda e: e.tensor_copy(tint[:, 0:n], src), waits=waits)
        t = vo(lambda e: e.tensor_copy(a, tint[:, 0:n]), waits=(t,))
        t = vo(lambda e, s=src: e.tensor_tensor(s, s, a, ALU.subtract), waits=(t,))
        t = vo(lambda e, s=src: e.scalar_tensor_tensor(a, s, 0.5, a, ALU.is_gt, ALU.add), waits=(t,))
        t = vo(lambda e, s=src: e.tensor_scalar(s, s, -0.5, -1.0, ALU.is_lt, ALU.mult), waits=(t,))
        return vo(lambda e, s=src, d=dst: e.scalar_tensor_tensor(d, s, 0.0, a, ALU.add, ALU.add),
                  waits=(t,))

    ld = []
    for dst, src in ((h128, h128_d), (idn, idn_d), (betap, beta_xd),
                     (betar, beta_rd), (l0gs, l0g_d), (l0bs, l0b_d),
                     (l1gs, l1g_d), (l1bs, l1b_d), (bouts, b_out_d),
                     (sflag, scanflag_d), (snapf, snapf_d), (cflag, cflags_d)):
        ld.append(p.dma(SYN, dst[:, :], src[:, :]))
    for c in range(NCH):
        ld.append(p.dma(GPS, xres[:, c * TC:(c + 1) * TC],
                        x_in[c * 128:(c + 1) * 128, :]))
    ld.append(p.dma(SYN, wgt16[:, :], kpj0(wgateT[:, :])))
    ldall = tuple(ld)
    tok_wgt = ldall[-1]

    tz = vo(lambda e: e.memset(fencet[:, :], 0.0))
    tz = vo(lambda e: e.memset(onec[:, :], 1.0), waits=(tz,))
    tz = vo(lambda e: e.memset(onerow[:, :], 1.0), waits=(tz,))
    tz = vo(lambda e: e.memset(gzero[:, :], 0.0), waits=(tz,))
    tz = vo(lambda e: e.memset(scanstate[:, :], 0.0), waits=(tz,))

    tk = so(lambda e: e.activation(lamx[:, :], betap[:, :], ACT.Exp, scale=-LN2), waits=ldall)
    tk = vo(lambda e: e.tensor_scalar(lamx[:, :], lamx[:, :], -1.0, 1.0, ALU.mult, ALU.add), waits=(tk,))
    tok_lam = vo(lambda e: e.tensor_scalar(lamx[:, :], lamx[:, :], TINY, 1.0 - 1e-9, ALU.max, ALU.min), waits=(tk,))
    tk = so(lambda e: e.activation(lnlr[:, :], betar[:, :], ACT.Exp, scale=-LN2), waits=ldall)
    tk = vo(lambda e: e.tensor_scalar(lnlr[:, :], lnlr[:, :], -1.0, 1.0, ALU.mult, ALU.add), waits=(tk,))
    tk = vo(lambda e: e.tensor_scalar(lnlr[:, :], lnlr[:, :], TINY, 1.0 - 1e-9, ALU.max, ALU.min), waits=(tk,))
    tk = so(lambda e: e.activation(lnlr[:, :], lnlr[:, :], ACT.Ln), waits=(tk,))
    tok_lnl = tk
    tok_l1024 = so(lambda e: e.activation(lam1024[:, :], lnlr[:, :], ACT.Exp, scale=1024.0), waits=(tk,))

    tok_drow = tok_lnl
    for hh in range(NH):
        t = p.dma(SYN, rowbuf[:, 0:TC], scanE_d[hh:hh + 1, :], waits=(tok_drow,),
                  sem="ddr")
        t = vo(lambda e, h_=hh: e.tensor_scalar(rowbuf[:, 0:TC], rowbuf[:, 0:TC],
                                                lnlr[0:1, h_:h_ + 1], None, ALU.mult),
               waits=(t, tok_lnl))
        t = so(lambda e: e.activation(rowbuf[:, 0:TC], rowbuf[:, 0:TC], ACT.Exp), waits=(t,))
        tok_drow = p.dma(GPS, drow_d[hh:hh + 1, :], rowbuf[:, 0:TC], waits=(t,),
                         sem="dds")

    def ternarize(w_src, n_slabs, dst, scal_idx, prev_tok):
        nrg = w_src.shape[0] // D
        subs = [(i, rg) for i in range(n_slabs) for rg in range(nrg)]
        tacc = None
        tld_prev = {0: prev_tok, 1: prev_tok}
        for s, (i, rg) in enumerate(subs):
            buf = wfa0 if s % 2 == 0 else wfb0
            w = (tld_prev[s % 2],) if tld_prev[s % 2] else ()
            tl = p.dma(SYN, buf.rearrange("p (k j) -> p k j", j=128),
                       kpj(w_src[rg * D:(rg + 1) * D, i * 128:(i + 1) * 128]),
                       waits=w, sem="dt%d" % (s % 2))
            tr_ = vo(lambda e, b=buf: e.tensor_reduce(t1[:, 0:1], b, AXX, ALU.add,
                                                      apply_absolute_value=True), waits=(tl,))
            if tacc is None:
                tacc = vo(lambda e: e.tensor_copy(accv[:, :], t1[:, 0:1]), waits=(tr_,))
            else:
                tacc = vo(lambda e: e.tensor_tensor(accv[:, :], accv[:, :], t1[:, 0:1], ALU.add),
                          waits=(tr_, tacc))
            tld_prev[i % 2] = tr_
        tm = mm(pb[6][0:1, 0:1], accv[:, :], onec[:, :], True, True, waits=bank_start(6, (tacc,)))
        bank_written(6, tm)
        tm = vo(lambda e: e.tensor_copy(sc1[:, :], pb[6][0:1, 0:1]), waits=(tm,))
        bank_read(6, tm)
        tm = vo(lambda e: e.tensor_scalar(sc1[:, :], sc1[:, :], 1.0 / float(D * FF), EPS_Q,
                                          ALU.mult, ALU.max), waits=(tm,))
        tm = vo(lambda e, si=scal_idx: e.tensor_copy(scw[:, si:si + 1], sc1[:, :]), waits=(tm,))
        tb = mm(pb[6][:, 0:1], onerow[:, :], sc1[:, :], True, True, waits=bank_start(6, (tm,)))
        bank_written(6, tb)
        tthr = vo(lambda e: e.tensor_scalar(wscol[:, :], pb[6][:, 0:1], 0.5, None, ALU.mult),
                  waits=(tb,))
        bank_read(6, tthr)
        tprev = {0: tthr, 1: tthr}
        last = None
        for s, (i, rg) in enumerate(subs):
            buf = wfa0 if s % 2 == 0 else wfb0
            obuf = wba if s % 2 == 0 else wbb
            tl = p.dma(SYN, buf.rearrange("p (k j) -> p k j", j=128),
                       kpj(w_src[rg * D:(rg + 1) * D, i * 128:(i + 1) * 128]),
                       waits=(tprev[s % 2],), sem="dt%d" % (s % 2))
            ta = None
            for hf in range(2):
                bh = buf[:, hf * 1024:(hf + 1) * 1024]
                oh = obuf[:, hf * 1024:(hf + 1) * 1024]
                ta1 = vo(lambda e, b=bh: e.tensor_scalar(t1[:, :], b, wscol[:, 0:1], None,
                                                         ALU.is_ge), waits=(tl, tthr))
                ta2 = vo(lambda e, b=bh: e.tensor_scalar(t2[:, :], b, -1.0, None, ALU.mult),
                         waits=(tl,))
                ta2 = vo(lambda e: e.tensor_scalar(t2[:, :], t2[:, :], wscol[:, 0:1], -1.0,
                                                   ALU.is_ge, ALU.mult), waits=(ta2,))
                ta = vo(lambda e, o=oh: e.tensor_tensor(o, t1[:, :], t2[:, :], ALU.add),
                        waits=(ta1, ta2))
            td = p.dma(GPS, dst[i:i + 1, rg * D * 128:(rg + 1) * D * 128]
                       .rearrange("a (k p j) -> a k p j", p=128, j=128)
                       .rearrange("a k p j -> a p k j"),
                       obuf.rearrange("p (k j) -> p k j", j=128), waits=(ta,))
            tprev[s % 2] = ta
            last = td
        return last

    need_ffn = ("p1" in phases) or ("p3" in phases)
    if need_ffn:
        t = ternarize(wgf, NFH, terng, 0, tz)
        t = ternarize(wuf, NFH, ternu, 1, t)
        ternarize(wdf, NCH, ternd, 2, t)

    p.barrier()

    def ln_stats(tt0, waits=()):
        tks = []
        for c in range(NCH):
            xc = xres[:, c * TC + tt0:c * TC + tt0 + TT]
            w4 = bank_start(4, waits) if c == 0 else ()
            w5 = bank_start(5) if c == 0 else ()
            m1 = mm(pb[4][0:1, 0:TT], onec[:, :], xc, c == 0, c == NCH - 1,
                    waits=w4 + (waits if c == 0 else ()))
            tsq = so(lambda e, x=xc: e.activation(t1[:, 0:TT], x, ACT.Square),
                     waits=((tks[-1],) if tks else ()) + (waits if c == 0 else ()))
            m2 = mm(pb[5][0:1, 0:TT], onec[:, :], t1[:, 0:TT], c == 0, c == NCH - 1,
                    waits=(tsq,) + (w5 if c == 0 else ()))
            tks.append(m2)
        bank_written(4, m1)
        bank_written(5, m2)
        tmu = vo(lambda e: e.tensor_scalar(mur[:, :], pb[4][0:1, 0:TT], 1.0 / D, None, ALU.mult),
                 waits=(m1, m2))
        bank_read(4, tmu)
        tsq2 = vo(lambda e: e.tensor_scalar(sqr[:, :], pb[5][0:1, 0:TT], 1.0 / D, None, ALU.mult),
                  waits=(m2,))
        bank_read(5, tsq2)
        tv = vo(lambda e: e.tensor_tensor(r1[:, :], mur[:, :], mur[:, :], ALU.mult), waits=(tmu,))
        tv = vo(lambda e: e.tensor_tensor(r1[:, :], sqr[:, :], r1[:, :], ALU.subtract), waits=(tv, tsq2))
        tv = vo(lambda e: e.tensor_scalar(r1[:, :], r1[:, :], EPS_LN, None, ALU.add), waits=(tv,))
        ts_ = so(lambda e: e.activation(r1[:, :], r1[:, :], ACT.Sqrt), waits=(tv,))
        tv = vo(lambda e: e.reciprocal(rsr[:, :], r1[:, :]), waits=(ts_,))
        return tv

    def bcast(bank, row, waits=()):
        t = mm(pb[bank][:, 0:TT], onerow[:, :], row, True, True,
               waits=bank_start(bank, waits))
        bank_written(bank, t)
        dst = bc6 if bank == 6 else bc7
        t = vo(lambda e, d=dst, b=bank: e.tensor_copy(d[:, :], pb[b][:, 0:TT]), waits=(t,))
        bank_read(bank, t)
        return t

    def colmax_row(src, dst_row, waits=()):
        last = None
        for half in range(TT // 128):
            sl = src[:, half * 128:(half + 1) * 128]
            t = tr(pb[3][:, 0:128], sl, waits=bank_start(3, waits))
            bank_written(3, t)
            t = vo(lambda e: e.tensor_reduce(t1[:, 0:1], pb[3][:, 0:128], AXX, ALU.max),
                   waits=(t,))
            bank_read(3, t)
            t2_ = tr(pb[3][0:1, 128:256], t1[:, 0:1], waits=bank_start(3, (t,)))
            bank_written(3, t2_)
            last = vo(lambda e, h=half: e.tensor_copy(dst_row[0:1, h * 128:(h + 1) * 128],
                                                      pb[3][0:1, 128:256]), waits=(t2_,))
            bank_read(3, last)
        return last

    def center_tile(tt0, dst, apply_rsig, waits=(), dstride=TT, dbase=0):
        tmub = bcast(6, mur[0:1, :], waits=waits)
        trb = bcast(7, rsr[0:1, :], waits=waits) if apply_rsig else None
        last = None
        for c in range(NCH):
            xc = xres[:, c * TC + tt0:c * TC + tt0 + TT]
            dc = dst[:, c * dstride + dbase:c * dstride + dbase + TT]
            t = vo(lambda e, x=xc, d=dc: e.tensor_tensor(
                d, x, bc6[:, :], ALU.subtract), waits=(tmub,))
            bank_read(6, t)
            if apply_rsig:
                t = vo(lambda e, d=dc: e.tensor_tensor(d, d, bc7[:, :], ALU.mult),
                       waits=(t, trb))
                bank_read(7, t)
            last = t
        return last

    def emit_l0_ffn():
        tail = ()
        for it in range(NTT):
            tt0 = it * TT
            trs = ln_stats(tt0, waits=tail)
            txc = center_tile(tt0, xcen, False, waits=(trs,))
            tmax = None
            tsq_prev = txc
            for c in range(NCH):
                tsq = so(lambda e, c_=c: e.activation(t1[:, 0:TT], xcen[:, c_ * TT:(c_ + 1) * TT],
                                                      ACT.Square), waits=(txc, tsq_prev))
                if tmax is None:
                    tmax = vo(lambda e: e.tensor_copy(t2[:, 0:TT], t1[:, 0:TT]), waits=(tsq,))
                else:
                    tmax = vo(lambda e: e.tensor_tensor(t2[:, 0:TT], t2[:, 0:TT], t1[:, 0:TT],
                                                        ALU.max), waits=(tsq, tmax))
                tsq_prev = tmax
            tmax = colmax_row(t2[:, 0:TT], r2, waits=(tmax,))
            tmax = so(lambda e: e.activation(r2[:, :], r2[:, :], ACT.Sqrt), waits=(tmax,))
            tmax = vo(lambda e: e.tensor_tensor(r2[:, :], r2[:, :], rsr[:, :], ALU.mult), waits=(tmax, trs))
            tmax = vo(lambda e: e.tensor_scalar(r2[:, :], r2[:, :], EPS_Q, None, ALU.max), waits=(tmax,))
            tsi = vo(lambda e: e.tensor_scalar(r3[:, :], r2[:, :], 1.0 / 128.0, None, ALU.mult), waits=(tmax,))
            ts1 = vo(lambda e: e.reciprocal(r4[:, :], r2[:, :]), waits=(tmax,))
            ts1 = vo(lambda e: e.tensor_scalar(r4[:, :], r4[:, :], 128.0, None, ALU.mult), waits=(ts1,))
            ts1 = vo(lambda e: e.tensor_tensor(r4[:, :], r4[:, :], rsr[:, :], ALU.mult), waits=(ts1,))
            tgsc = vo(lambda e: e.tensor_scalar(r5[:, :], r3[:, :], scw[0:1, 0:1], None, ALU.mult),
                      waits=(tsi,))
            trb = bcast(7, r4[0:1, :], waits=(ts1,))
            tq = txc
            for g in range(4):
                src = xcen[:, g * 1024:(g + 1) * 1024]
                tmul = vo(lambda e, s=src: e.tensor_tensor(
                    s, s, bc7[:, :].rearrange("p (o t) -> p o t", o=1).to_broadcast([128, 4, TT]),
                    ALU.mult), waits=(trb, tq))
                bank_read(7, tmul)
                trd = emit_round(t1[:, :], src, waits=(tmul,))
                tq = vo(lambda e, g_=g: e.tensor_scalar(xqs[:, g_ * 1024:(g_ + 1) * 1024],
                                                        t1[:, :], 127.0, None, ALU.min),
                        waits=(trd,))
            tgb = bcast(6, r5[0:1, :], waits=(tgsc,))
            wtok = {("g", 0): None, ("g", 1): None, ("u", 0): None, ("u", 1): None}
            te = tq
            for i in range(NFH):
                gbuf = wga if i % 2 == 0 else wgb
                ubuf = wua if i % 2 == 0 else wub
                wg_ = p.dma(SYN, gbuf[:, :], terng[i:i + 1, :]
                            .rearrange("a (k p j) -> a k p j", p=128, j=128)
                            .rearrange("a k p j -> p a k j"),
                            waits=(wtok[("g", i % 2)],) if wtok[("g", i % 2)] else (),
                            sem="dg%d" % (i % 2))
                wu_ = p.dma(SYN, ubuf[:, :], ternu[i:i + 1, :]
                            .rearrange("a (k p j) -> a k p j", p=128, j=128)
                            .rearrange("a k p j -> p a k j"),
                            waits=(wtok[("u", i % 2)],) if wtok[("u", i % 2)] else (),
                            sem="du%d" % (i % 2))
                bg, bu = (0, 2) if i % 2 == 0 else (1, 3)
                mg = mu_ = None
                for kc in range(NCH):
                    wg2 = bank_start(bg, (wg_, tq)) if kc == 0 else (wg_, tq)
                    wu2 = bank_start(bu, (wu_, tq)) if kc == 0 else (wu_, tq)
                    mg = mm(pb[bg][:, 0:TT], gbuf[:, kc * 128:(kc + 1) * 128],
                            xqs[:, kc * TT:(kc + 1) * TT], kc == 0, kc == NCH - 1, waits=wg2)
                    mu_ = mm(pb[bu][:, 0:TT], ubuf[:, kc * 128:(kc + 1) * 128],
                             xqs[:, kc * TT:(kc + 1) * TT], kc == 0, kc == NCH - 1, waits=wu2)
                bank_written(bg, mg)
                bank_written(bu, mu_)
                te = vo(lambda e, b=bg: e.tensor_tensor(t1[:, 0:TT], pb[b][:, 0:TT],
                                                        bc6[:, :], ALU.mult),
                        waits=(mg, tgb, te))
                bank_read(bg, te)
                bank_read(6, te)
                te = so(lambda e: e.activation(t1[:, 0:TT], t1[:, 0:TT], ACT.Silu), waits=(te,))
                te = vo(lambda e, b=bu, i_=i: e.tensor_tensor(slab[:, i_ * TT:(i_ + 1) * TT],
                                                              t1[:, 0:TT], pb[b][:, 0:TT],
                                                              ALU.mult), waits=(te, mu_))
                bank_read(bu, te)
                wtok[("g", i % 2)] = te
                wtok[("u", i % 2)] = te
            bt = te
            for lev in range(6):
                stride = 32 >> lev
                A = slab[:, :].rearrange("p (b two st) -> p b two st", two=2,
                                         st=stride * TT)[:, :, 0, :]
                Bp = slab[:, :].rearrange("p (b two st) -> p b two st", two=2,
                                          st=stride * TT)[:, :, 1, :]
                bt = vo(lambda e, a=A, b=Bp: e.tensor_tensor(a, a, b, ALU.add), waits=(bt,))
                bt = vo(lambda e, a=A, b=Bp: e.scalar_tensor_tensor(b, b, -2.0, a, ALU.mult,
                                                                    ALU.add), waits=(bt,))
            tmax = None
            tprev = bt
            for i in range(NFH):
                ci = slab[:, i * TT:(i + 1) * TT]
                th = vo(lambda e, c=ci: e.tensor_copy(hil[:, :], c), waits=(tprev,))
                th = vo(lambda e, c=ci: e.tensor_tensor(lol[:, :], c, hil[:, :], ALU.subtract),
                        waits=(th,))
                bk = i % 2
                w0 = bank_start(bk, (th,))
                mh = mm(pb[bk][:, 0:TT], h128[:, :], hil[:, :], True, False, waits=w0)
                mh = mm(pb[bk][:, 0:TT], h128[:, :], lol[:, :], False, True, waits=(th,))
                bank_written(bk, mh)
                tev = vo(lambda e, c=ci, b=bk: e.tensor_copy(c, pb[b][:, 0:TT]), waits=(mh,))
                bank_read(bk, tev)
                tsq = so(lambda e, c=ci: e.activation(t1[:, 0:TT], c, ACT.Square),
                         waits=(tev,) if tmax is None else (tev, tmax))
                if tmax is None:
                    tmax = vo(lambda e: e.tensor_copy(t2[:, 0:TT], t1[:, 0:TT]), waits=(tsq,))
                else:
                    tmax = vo(lambda e: e.tensor_tensor(t2[:, 0:TT], t2[:, 0:TT], t1[:, 0:TT],
                                                        ALU.max), waits=(tsq, tmax))
                tprev = tsq
            tmax = colmax_row(t2[:, 0:TT], r2, waits=(tmax,))
            tmax = so(lambda e: e.activation(r2[:, :], r2[:, :], ACT.Sqrt), waits=(tmax,))
            tu = vo(lambda e: e.tensor_scalar(r5[:, :], r3[:, :], scw[0:1, 1:2], None, ALU.mult),
                    waits=(tsi,))
            tu = vo(lambda e: e.tensor_scalar(r5[:, :], r5[:, :], C_FWHT, None, ALU.mult), waits=(tu,))
            tmax = vo(lambda e: e.tensor_tensor(r2[:, :], r2[:, :], r5[:, :], ALU.mult), waits=(tmax, tu))
            tmax = vo(lambda e: e.tensor_scalar(r2[:, :], r2[:, :], EPS_Q, None, ALU.max), waits=(tmax,))
            ts2 = vo(lambda e: e.reciprocal(r4[:, :], r2[:, :]), waits=(tmax,))
            ts2 = vo(lambda e: e.tensor_tensor(r4[:, :], r4[:, :], r5[:, :], ALU.mult), waits=(ts2,))
            ts2 = vo(lambda e: e.tensor_scalar(r4[:, :], r4[:, :], 128.0, None, ALU.mult), waits=(ts2,))
            tds = vo(lambda e: e.tensor_scalar(r5[:, :], r2[:, :], scw[0:1, 2:3], None, ALU.mult),
                     waits=(tmax,))
            tds = vo(lambda e: e.tensor_scalar(r5[:, :], r5[:, :], 0.5 / 128.0, None, ALU.mult),
                     waits=(tds,))
            trb2 = bcast(7, r4[0:1, :], waits=(ts2,))
            tq2 = tprev
            for g in range(NFH // 4):
                src = slab[:, g * 1024:(g + 1) * 1024]
                tmul = vo(lambda e, s=src: e.tensor_tensor(
                    s, s, bc7[:, :].rearrange("p (o t) -> p o t", o=1).to_broadcast([128, 4, TT]),
                    ALU.mult), waits=(trb2, tq2))
                bank_read(7, tmul)
                trd = emit_round(t1[:, :], src, waits=(tmul,))
                tq2 = vo(lambda e, g_=g: e.tensor_scalar(xq2[:, g_ * 1024:(g_ + 1) * 1024],
                                                         t1[:, :], 127.0, None, ALU.min),
                         waits=(trd,))
            tdb = bcast(6, r5[0:1, :], waits=(tds,))
            wdt = {0: tq2, 1: tq2}
            tlast = tq2
            for oc in range(NCH):
                dbuf = wda if oc % 2 == 0 else wdb
                wdl = p.dma(SYN, dbuf[:, :], ternd[oc:oc + 1, :]
                            .rearrange("a (k p j) -> a k p j", p=128, j=128)
                            .rearrange("a k p j -> p a k j"),
                            waits=(wdt[oc % 2],), sem="dd%d" % (oc % 2))
                bk = oc % 2
                md = None
                for kc in range(NFH):
                    w0 = bank_start(bk, (wdl, tq2)) if kc == 0 else (wdl, tq2)
                    md = mm(pb[bk][:, 0:TT], dbuf[:, kc * 128:(kc + 1) * 128],
                            xq2[:, kc * TT:(kc + 1) * TT], kc == 0, kc == NFH - 1, waits=w0)
                bank_written(bk, md)
                te2 = vo(lambda e, b=bk: e.tensor_tensor(t1[:, 0:TT], pb[b][:, 0:TT],
                                                         bc6[:, :], ALU.mult),
                         waits=(md, tdb, tlast))
                bank_read(bk, te2)
                bank_read(6, te2)
                xc = xres[:, oc * TC + tt0:oc * TC + tt0 + TT]
                te2 = vo(lambda e, x=xc: e.tensor_tensor(x, x, t1[:, 0:TT], ALU.add), waits=(te2,))
                wdt[oc % 2] = te2
                tlast = te2
            tail = (wdt[0], wdt[1])
            p.barrier()

    def emit_l0_attn():
        wtok = {0: None, 1: None}
        tscan = None
        tcontrib = None
        tz0 = vo(lambda e: e.memset(scanstate[:, :], 0.0))
        tail = ()
        for it in range(NTT):
            tt0 = it * TT
            trs = ln_stats(tt0, waits=tail)
            tail = (trs,)
            txc = center_tile(tt0, xcen, True, waits=(trs,))
            tsf = vo(lambda e, i=it: e.tensor_scalar(scanstate[:, :], scanstate[:, :],
                                                     sflag[:, i:i + 1], None, ALU.mult),
                     waits=(txc, tz0) if tscan is None else (txc, tscan))
            for c in range(NCH):
                buf = wfa if c % 2 == 0 else wfb
                wl = p.dma(SYN, buf.rearrange("p (k j) -> p k j", j=128),
                           kpj(wvT[:, c * 128:(c + 1) * 128]), sem="dw%d" % (c % 2),
                           waits=(wtok[c % 2],) if wtok[c % 2] else ())
                bk = c % 2
                mv = None
                for kc in range(NCH):
                    w0 = bank_start(bk, (wl, txc)) if kc == 0 else (wl, txc)
                    mv = mm(pb[bk][:, 0:TT], buf[:, kc * 128:(kc + 1) * 128],
                            xcen[:, kc * TT:(kc + 1) * TT], kc == 0, kc == NCH - 1, waits=w0)
                bank_written(bk, mv)
                lam_b = lamx[:, c:c + 1].to_broadcast([128, TT])
                tsc = vo(lambda e, b=bk, c_=c, lb=lam_b, o=tt0: e.tensor_tensor_scan(
                    retv[:, c_ * TC + o:c_ * TC + o + TT], lb, pb[b][:, 0:TT],
                    scanstate[:, c_:c_ + 1], ALU.mult, ALU.add), waits=(mv, tsf, tok_lam))
                bank_read(bk, tsc)
                wtok[c % 2] = tsc
            tscan = vo(lambda e, o=tt0: e.tensor_copy(
                scanstate[:, :],
                retv[:, :].rearrange("p (c t) -> p c t", c=NCH)[:, :, o + TT - 1]),
                waits=(wtok[0], wtok[1]))
            if it == 1:
                tcontrib = vo(lambda e: e.tensor_scalar(gtok[:, 0:NH], scanstate[:, :],
                                                        snapf[:, 0:1], None, ALU.mult),
                              waits=(tscan,))
            if it == 3:
                tcontrib = vo(lambda e: e.scalar_tensor_tensor(gtok[:, 0:NH], scanstate[:, :],
                                                               snapf[:, 1:2], gtok[:, 0:NH],
                                                               ALU.mult, ALU.add),
                              waits=(tscan, tcontrib))
        if P2_STOP == "scan":
            p.barrier()
            return
        tsn = p.dma(GPS, ccs_in[:, :], gtok[:, 0:NH], waits=(tcontrib,))
        tcc = p.op(GPS, lambda e: e.collective_compute(
            "AllGather", ALU.bypass, replica_groups=RG,
            ins=[ccs_in[:, :].opt()], outs=[ccs_out[:, :].opt()]),
            waits=(tsn,), inc="scc")
        ta0 = p.dma(GPS, agt0[:, :],
                    ccs_out[0:1, :].rearrange("a (pp c) -> (a pp) c", pp=128), waits=(tcc,),
                    sem="da0")
        ta1 = p.dma(GPS, agt1[:, :],
                    ccs_out[1:2, :].rearrange("a (pp c) -> (a pp) c", pp=128), waits=(tcc,),
                    sem="da1")
        if P2_STOP == "ag":
            p.wait(GPS, ta0); p.wait(GPS, ta1)
            p.barrier()
            return
        tl = so(lambda e: e.activation(t1[:, 0:NH], lamx[:, :], ACT.Ln),
                waits=(tok_lam, tscan) + tail)
        tl = so(lambda e: e.activation(t1[:, 0:NH], t1[:, 0:NH], ACT.Exp, scale=1024.0),
                waits=(tl,))
        tl = vo(lambda e: e.tensor_scalar(t1[:, 0:NH], t1[:, 0:NH], cfl3[:, 0:1],
                                          cfl1[:, 0:1], ALU.mult, ALU.add), waits=(tl,))
        tco = vo(lambda e: e.tensor_tensor(gtok[:, 0:NH], agt0[:, :], t1[:, 0:NH], ALU.mult),
                 waits=(tl, ta0))
        tco = vo(lambda e: e.scalar_tensor_tensor(gtok[:, 0:NH], agt1[:, :], cflb[:, 0:1],
                                                  gtok[:, 0:NH], ALU.mult, ALU.add),
                 waits=(tco, ta1))
        tprev = tco
        for hh in range(NH):
            w3 = bank_start(3, (tco,))
            tt_ = tr(pb[3][0:1, 0:128], gtok[:, hh:hh + 1], waits=w3)
            bank_written(3, tt_)
            tt_ = vo(lambda e: e.tensor_copy(rowbuf[0:1, 0:128], pb[3][0:1, 0:128]), waits=(tt_,))
            bank_read(3, tt_)
            tdr = p.dma(SYN, rowb2[0:1, 0:TC], drow_d[hh:hh + 1, :], waits=(tprev,),
                        sem="ddr")
            for sg2 in range(2):
                w4 = bank_start(4, (tt_, tdr))
                m_ = mm(pb[4][:, 0:512], rowbuf[0:1, 0:128],
                        rowb2[0:1, sg2 * 512:(sg2 + 1) * 512], True, True, waits=w4)
                bank_written(4, m_)
                tprev = vo(lambda e, h_=hh, s=sg2: e.tensor_tensor(
                    retv[:, h_ * TC + s * 512:h_ * TC + (s + 1) * 512],
                    retv[:, h_ * TC + s * 512:h_ * TC + (s + 1) * 512],
                    pb[4][:, 0:512], ALU.add), waits=(m_, tscan))
                bank_read(4, tprev)
        p.barrier()
        if P2_STOP == "corr":
            return
        wtok2 = {0: None, 1: None}
        for it in range(NTT):
            tt0 = it * TT
            trs = ln_stats(tt0)
            txc = center_tile(tt0, xcen, True, waits=(trs,))
            tq_all = []
            for c in range(NCH):
                buf = wfa if c % 2 == 0 else wfb
                wl = p.dma(SYN, buf.rearrange("p (k j) -> p k j", j=128),
                           kpj(wqT[:, c * 128:(c + 1) * 128]), sem="dw%d" % (c % 2),
                           waits=(wtok2[c % 2],) if wtok2[c % 2] else ())
                bk = c % 2
                mq = None
                for kc in range(NCH):
                    w0 = bank_start(bk, (wl, txc)) if kc == 0 else (wl, txc)
                    mq = mm(pb[bk][:, 0:TT], buf[:, kc * 128:(kc + 1) * 128],
                            xcen[:, kc * TT:(kc + 1) * TT], kc == 0, kc == NCH - 1, waits=w0)
                bank_written(bk, mq)
                rc = retv[:, c * TC + tt0:c * TC + tt0 + TT]
                tqr = vo(lambda e, b=bk, r=rc: e.tensor_tensor(r, pb[b][:, 0:TT], r, ALU.mult),
                         waits=(mq,))
                bank_read(bk, tqr)
                wtok2[c % 2] = tqr
                tq_all.append(tqr)
            tq_all = tuple(tq_all[-2:])
            if P2_STOP == "b1":
                p.barrier()
                continue
            tks = []
            for c in range(NCH):
                rc = retv[:, c * TC + tt0:c * TC + tt0 + TT]
                w4 = bank_start(4, tq_all) if c == 0 else ()
                w5 = bank_start(5, tq_all) if c == 0 else ()
                m1 = mm(pb[4][0:1, 0:TT], onec[:, :], rc, c == 0, c == NCH - 1,
                        waits=w4 + tq_all)
                tsq = so(lambda e, r=rc: e.activation(t1[:, 0:TT], r, ACT.Square),
                         waits=tq_all + ((tks[-1],) if tks else ()))
                m2 = mm(pb[5][0:1, 0:TT], onec[:, :], t1[:, 0:TT], c == 0, c == NCH - 1,
                        waits=(tsq,) + (w5 if c == 0 else ()))
                tks.append(m2)
            bank_written(4, m1)
            bank_written(5, m2)
            tmu = vo(lambda e: e.tensor_scalar(mur[:, :], pb[4][0:1, 0:TT], 1.0 / D, None,
                                               ALU.mult), waits=(m1, m2))
            bank_read(4, tmu)
            tsq2 = vo(lambda e: e.tensor_scalar(sqr[:, :], pb[5][0:1, 0:TT], 1.0 / D, None,
                                                ALU.mult), waits=(m2,))
            bank_read(5, tsq2)
            tv = vo(lambda e: e.tensor_tensor(r1[:, :], mur[:, :], mur[:, :], ALU.mult), waits=(tmu,))
            tv = vo(lambda e: e.tensor_tensor(r1[:, :], sqr[:, :], r1[:, :], ALU.subtract),
                    waits=(tv, tsq2))
            tv = vo(lambda e: e.tensor_scalar(r1[:, :], r1[:, :], EPS_LN, None, ALU.add), waits=(tv,))
            ts_ = so(lambda e: e.activation(r1[:, :], r1[:, :], ACT.Sqrt), waits=(tv,))
            tv = vo(lambda e: e.reciprocal(rsr[:, :], r1[:, :]), waits=(ts_,))
            if P2_STOP == "b2a":
                p.barrier()
                continue
            tmu2 = bcast(6, mur[0:1, :], waits=(tmu, tv))
            trs2 = bcast(7, rsr[0:1, :], waits=(tv,))
            tln = None
            for c in range(NCH):
                rc = retv[:, c * TC + tt0:c * TC + tt0 + TT]
                t_ = vo(lambda e, r=rc: e.tensor_tensor(r, r, bc6[:, :], ALU.subtract),
                        waits=(tmu2,))
                bank_read(6, t_)
                t_ = vo(lambda e, r=rc: e.tensor_tensor(r, r, bc7[:, :], ALU.mult),
                        waits=(t_, trs2))
                bank_read(7, t_)
                tln = vo(lambda e, r=rc, c_=c: e.tensor_scalar(r, r, l0gs[:, c_:c_ + 1],
                                                               l0bs[:, c_:c_ + 1], ALU.mult,
                                                               ALU.add), waits=(t_,))
            if P2_STOP == "b2":
                p.barrier()
                continue
            tg_all = []
            prev_g = tln
            for c in range(NCH):
                buf = wfa if c % 2 == 0 else wfb
                wl = p.dma(SYN, buf.rearrange("p (k j) -> p k j", j=128),
                           kpj(wgT[:, c * 128:(c + 1) * 128]), sem="dw%d" % (c % 2),
                           waits=(wtok2[c % 2],) if wtok2[c % 2] else ())
                bk = c % 2
                mg = None
                for kc in range(NCH):
                    w0 = bank_start(bk, (wl, txc)) if kc == 0 else (wl, txc)
                    mg = mm(pb[bk][:, 0:TT], buf[:, kc * 128:(kc + 1) * 128],
                            xcen[:, kc * TT:(kc + 1) * TT], kc == 0, kc == NCH - 1, waits=w0)
                bank_written(bk, mg)
                ts_ = so(lambda e, b=bk: e.activation(t1[:, 0:TT], pb[b][:, 0:TT], ACT.Silu),
                         waits=(mg, prev_g))
                bank_read(bk, ts_)
                rc = retv[:, c * TC + tt0:c * TC + tt0 + TT]
                tg2 = vo(lambda e, r=rc: e.tensor_tensor(r, r, t1[:, 0:TT], ALU.mult),
                         waits=(ts_, tln))
                wtok2[c % 2] = tg2
                prev_g = tg2
                tg_all.append(tg2)
            tg_all = tuple(tg_all[-2:])
            if P2_STOP == "b3":
                p.barrier()
                continue
            for c in range(NCH):
                buf = wfa if c % 2 == 0 else wfb
                wl = p.dma(SYN, buf.rearrange("p (k j) -> p k j", j=128),
                           kpj(woT[:, c * 128:(c + 1) * 128]), sem="dw%d" % (c % 2),
                           waits=(wtok2[c % 2],) if wtok2[c % 2] else ())
                bk = c % 2
                mo = None
                for kc in range(NCH):
                    w0 = bank_start(bk, (wl,) + tg_all) if kc == 0 else (wl,) + tg_all
                    mo = mm(pb[bk][:, 0:TT], buf[:, kc * 128:(kc + 1) * 128],
                            retv[:, kc * TC + tt0:kc * TC + tt0 + TT], kc == 0, kc == NCH - 1,
                            waits=w0)
                bank_written(bk, mo)
                xc = xres[:, c * TC + tt0:c * TC + tt0 + TT]
                to_ = vo(lambda e, x=xc, b=bk: e.tensor_tensor(x, x, pb[b][:, 0:TT], ALU.add),
                         waits=(mo,))
                bank_read(bk, to_)
                wtok2[c % 2] = to_
            tail = (wtok2[0], wtok2[1])
            p.barrier()

    def emit_l1_ffn():
        tgate = tuple()
        txls = []
        tail = ()
        for it in range(NTT):
            tt0 = it * TT
            trs = ln_stats(tt0, waits=tail)
            txl = center_tile(tt0, xlnA, True, waits=(trs,), dstride=TC, dbase=tt0)
            txls.append(txl)
            for half in range(2):
                lt = it * 2 + half
                mgl = None
                for kc in range(NCH):
                    w2 = bank_start(2, (txl, tok_wgt)) if kc == 0 else ()
                    mgl = mm(pb[2][0:NH, 0:128], wgt16[:, kc * NH:(kc + 1) * NH],
                             xlnA[:, kc * TC + tt0 + half * 128:kc * TC + tt0 + half * 128 + 128],
                             kc == 0, kc == NCH - 1, waits=w2 + (txl,))
                bank_written(2, mgl)
                tcp = vo(lambda e: e.tensor_copy(t1[0:NH, 0:128], pb[2][0:NH, 0:128]),
                         waits=(mgl,))
                bank_read(2, tcp)
                w3 = bank_start(3, (tcp,))
                ttr = tr(pb[3][:, 0:NH], t1[0:NH, 0:128], waits=w3)
                bank_written(3, ttr)
                tgk = vo(lambda e: e.tensor_copy(gtok[:, 0:NH], pb[3][:, 0:NH]), waits=(ttr,))
                bank_read(3, tgk)
                t_ = vo(lambda e: e.tensor_reduce(t1[:, 0:1], gtok[:, 0:NH], AXX, ALU.max),
                        waits=(tgk,))
                t_ = vo(lambda e: e.tensor_scalar(t2[:, 0:NH], gtok[:, 0:NH], t1[:, 0:1], None,
                                                  ALU.subtract), waits=(t_,))
                t_ = so(lambda e: e.activation(t2[:, 0:NH], t2[:, 0:NH], ACT.Exp), waits=(t_,))
                t_ = vo(lambda e: e.tensor_reduce(t1[:, 1:2], t2[:, 0:NH], AXX, ALU.add),
                        waits=(t_,))
                t_ = vo(lambda e: e.reciprocal(t1[:, 2:3], t1[:, 1:2]), waits=(t_,))
                t_ = vo(lambda e: e.tensor_scalar(t2[:, 0:NH], t2[:, 0:NH], t1[:, 2:3], None,
                                                  ALU.mult), waits=(t_,))
                t_ = vo(lambda e, l=lt: e.tensor_reduce(gval[:, l:l + 1], t2[:, 0:NH], AXX,
                                                        ALU.max), waits=(t_,))
                t_ = vo(lambda e, l=lt: e.tensor_scalar(t2[:, 0:NH], t2[:, 0:NH],
                                                        gval[:, l:l + 1], None, ALU.is_ge),
                        waits=(t_,))
                t_ = vo(lambda e: e.tensor_tensor_scan(t2[:, NH:2 * NH], t2[:, 0:NH],
                                                       gzero[:, :], 0.0, ALU.add, ALU.add),
                        waits=(t_,))
                t_ = vo(lambda e: e.tensor_scalar(t2[:, NH:2 * NH], t2[:, NH:2 * NH], 1.5, None,
                                                  ALU.is_le), waits=(t_,))
                tg_last = vo(lambda e, l=lt: e.tensor_tensor(
                    gind[:, l * NH:(l + 1) * NH], t2[:, 0:NH], t2[:, NH:2 * NH], ALU.mult),
                    waits=(t_,))
                tgate = tgate + (tg_last,)
                tail = (tg_last,)
        tgate = tgate[-2:]
        tz1 = vo(lambda e: e.memset(kown[:, :], 0.0), waits=tgate)
        tz1 = vo(lambda e: e.memset(vown[:, :], 0.0), waits=(tz1,))
        wt = {0: None, 1: None}
        tsel = tz1
        for ob in range(16):
            wbuf = wkva if ob % 2 == 0 else wkvb
            wl = p.dma(SYN, wbuf.rearrange("p (k j) -> p k j", j=256),
                       kpj(wqkvT[:, D + ob * 256:D + (ob + 1) * 256]),
                       waits=(wt[ob % 2],) if wt[ob % 2] else (),
                       sem="dk%d" % (ob % 2))
            for lt in range(8):
                bk = lt % 2
                mkv = None
                for kc in range(NCH):
                    w0 = bank_start(bk, (wl,) + tuple(txls)) if kc == 0 else ()
                    mkv = mm(pb[bk][:, 0:256],
                             xlnA[:, kc * TC + lt * 128:kc * TC + lt * 128 + 128],
                             wbuf[:, kc * 256:(kc + 1) * 256], kc == 0, kc == NCH - 1,
                             waits=w0 + (wl,))
                bank_written(bk, mkv)
                for j in range(2):
                    col = ob * 256 + j * 128
                    isk = col < D
                    head = (col % D) // 128
                    dstt = kown if isk else vown
                    tsel = vo(lambda e, b=bk, j_=j, d=dstt, l=lt, h_=head: e.scalar_tensor_tensor(
                        d[:, l * 128:(l + 1) * 128], pb[b][:, j_ * 128:(j_ + 1) * 128],
                        gind[:, l * NH + h_:l * NH + h_ + 1],
                        d[:, l * 128:(l + 1) * 128], ALU.mult, ALU.add),
                        waits=(mkv, tsel))
                    bank_read(bk, tsel)
                wt[ob % 2] = tsel
        for lt in range(8):
            tsel = vo(lambda e, l=lt: e.tensor_scalar(vown[:, l * 128:(l + 1) * 128],
                                                      vown[:, l * 128:(l + 1) * 128],
                                                      gval[:, l:l + 1], None, ALU.mult),
                      waits=(tsel,))
        tdm = tsel
        for lt in range(8):
            tdm = p.dma(GPS, cckv_in[lt * 128:(lt + 1) * 128, :],
                        kown[:, lt * 128:(lt + 1) * 128], waits=(tdm,))
            tdm = p.dma(GPS, cckv_in[TC + lt * 128:TC + (lt + 1) * 128, :],
                        vown[:, lt * 128:(lt + 1) * 128], waits=(tdm,))
        tcc = p.op(GPS, lambda e: e.collective_compute(
            "AllGather", ALU.bypass, replica_groups=RG,
            ins=[cckv_in[:, :].opt()], outs=[cckv_out[:, :].opt()]),
            waits=(tdm,), inc="scc")

        def locate(g):
            if g < 4:
                return 0, g
            if g < 8:
                return 1, g - 4
            if g < 12:
                return 1, g - 8 + 4
            return 0, g - 12 + 4

        tkf = tcc
        for g in range(16):
            r, l2 = locate(g)
            tl = p.dma(GPS, t1[:, 0:128],
                       cckv_out[r:r + 1, l2 * 16384:(l2 + 1) * 16384]
                       .rearrange("a (pp d) -> (a pp) d", pp=128), waits=(tkf,))
            w3 = bank_start(3, (tl,))
            tt_ = tr(pb[3][:, 0:128], t1[:, 0:128], waits=w3)
            bank_written(3, tt_)
            tkf = vo(lambda e, g_=g: e.tensor_copy(kfm[:, g_ * 128:(g_ + 1) * 128],
                                                   pb[3][:, 0:128]), waits=(tt_,))
            bank_read(3, tkf)
            tkf = p.dma(GPS, vshg[:, g * 128:(g + 1) * 128],
                        cckv_out[r:r + 1, TC * 128 + l2 * 16384:TC * 128 + (l2 + 1) * 16384]
                        .rearrange("a (pp d) -> (a pp) d", pp=128), waits=(tkf,))
        tmn = vo(lambda e: e.memset(mnkr[:, :], 0.0), waits=(tkf,))
        mprev_ = None
        for sg2 in range(4):
            tsq = so(lambda e, s=sg2: e.activation(t1[:, 0:512], kfm[:, s * 512:(s + 1) * 512],
                                                   ACT.Square),
                     waits=(tkf,) if mprev_ is None else (tkf, mprev_))
            w4 = bank_start(4, (tsq,))
            m_ = mm(pb[4][0:1, 0:512], onec[:, :], t1[:, 0:512], True, True, waits=w4)
            bank_written(4, m_)
            mprev_ = m_
            t_ = vo(lambda e: e.tensor_copy(rowbuf[0:1, 0:512], pb[4][0:1, 0:512]), waits=(m_,))
            bank_read(4, t_)
            t_ = vo(lambda e: e.tensor_reduce(sc1[:, :], rowbuf[0:1, 0:512], AXX, ALU.max),
                    waits=(t_,))
            tmn = vo(lambda e: e.tensor_tensor(mnkr[:, :], mnkr[:, :], sc1[:, :], ALU.max),
                     waits=(t_, tmn))
        tmn = so(lambda e: e.activation(mnkr[:, :], mnkr[:, :], ACT.Sqrt), waits=(tmn,))
        qslab = RC[:, 4096:8192]
        wq2a = RC[:, 8192:9216]
        wq2b = RC[:, 9216:10240]
        pslb2 = RC[:, 8192:9216]
        masksb2 = RC[:, 9216:9856]
        grp_guard = (tkf,)
        for hg in range(4):
            wtq = {0: grp_guard, 1: grp_guard}
            qdone = []
            for hi in range(4):
                hh = hg * 4 + hi
                wla = p.dma(SYN, wq2a.rearrange("p (k j) -> p k j", j=128),
                            kpj(wqkvT[0:1024, hh * 128:(hh + 1) * 128]),
                            waits=wtq[0], sem="dw0")
                wlb = p.dma(SYN, wq2b.rearrange("p (k j) -> p k j", j=128),
                            kpj(wqkvT[1024:2048, hh * 128:(hh + 1) * 128]),
                            waits=wtq[1], sem="dw1")
                mqa_last = mqb_first = None
                for half in range(2):
                    bk = half
                    w0 = bank_start(bk, (wla,) + tuple(txls))
                    mq = None
                    for kc in range(NCH):
                        wbuf = wq2a if kc < 8 else wq2b
                        wdep = (wla,) if kc < 8 else (wlb,)
                        mq = mm(pb[bk][:, 0:512], wbuf[:, (kc % 8) * 128:(kc % 8) * 128 + 128],
                                xlnA[:, kc * TC + half * 512:kc * TC + half * 512 + 512],
                                kc == 0, kc == NCH - 1,
                                waits=(w0 + wdep) if kc == 0 else wdep)
                        if kc == 7:
                            mqa_last = mq
                        if kc == 8:
                            mqb_first = mq
                    bank_written(bk, mq)
                    dst = qslab.rearrange("p (s h t) -> p s h t", s=8, h=4)[
                        :, half * 4:half * 4 + 4, hi, :]
                    tqc = vo(lambda e, d=dst, b=bk: e.tensor_copy(
                        d, pb[b][:, 0:512].rearrange("p (s t) -> p s t", s=4)), waits=(mq,))
                    bank_read(bk, tqc)
                    qdone.append(tqc)
                wtq[0] = (mqa_last,)
                wtq[1] = (mq,)
            qdone = tuple(qdone)
            tmx = None
            for j in range(8):
                tsq = so(lambda e, j_=j: e.activation(
                    t1[:, 0:512], qslab[:, j_ * 512:(j_ + 1) * 512], ACT.Square),
                    waits=qdone if j == 0 else qdone + (tmx,))
                w4 = bank_start(4, (tsq,))
                m_ = mm(pb[4][0:1, 0:512], onec[:, :], t1[:, 0:512], True, True, waits=w4)
                bank_written(4, m_)
                t_ = vo(lambda e: e.tensor_copy(rowbuf[0:1, 0:512], pb[4][0:1, 0:512]),
                        waits=(m_,))
                bank_read(4, t_)
                if j == 0:
                    tmx = vo(lambda e: e.tensor_reduce(sc1[:, :], rowbuf[0:1, 0:512], AXX,
                                                       ALU.max), waits=(t_,))
                else:
                    t_ = vo(lambda e: e.tensor_reduce(rowbuf[0:1, 512:513], rowbuf[0:1, 0:512],
                                                      AXX, ALU.max), waits=(t_,))
                    tmx = vo(lambda e: e.tensor_tensor(sc1[:, :], sc1[:, :],
                                                       rowbuf[0:1, 512:513], ALU.max),
                             waits=(t_, tmx))
            t_ = so(lambda e: e.activation(sc1[:, :], sc1[:, :], ACT.Sqrt), waits=(tmx,))
            t_ = vo(lambda e: e.tensor_tensor(sc1[:, :], sc1[:, :], mnkr[:, :], ALU.mult),
                    waits=(t_, tmn))
            w5b = bank_start(5, (t_,))
            mb = mm(pb[5][:, 0:1], onerow[:, :], sc1[:, :], True, True, waits=w5b)
            bank_written(5, mb)
            tbias = vo(lambda e: e.tensor_scalar(t2[:, 0:1], pb[5][:, 0:1], -INV_SQRT_HD,
                                                 None, ALU.mult), waits=(mb,) + grp_guard)
            bank_read(5, tbias)
            prev_msk = qdone
            tfin = (tbias,)
            for sl in range(8):
                lt = 7 - sl
                nk = SCHED[sl]
                tmk = p.dma(SYN, masksb2[:, :], masks_d[sl * 128:(sl + 1) * 128, :],
                            waits=prev_msk + qdone, sem="dmk")
                qgrp = qslab[:, sl * 512:(sl + 1) * 512]
                msum_last = my_last = None
                psl_free = {0: (tbias,) + qdone, 1: (tbias,) + qdone}
                for kk in range(nk):
                    bqk = kk % 2
                    w0 = bank_start(bqk, ((tkf,) + qdone) if (sl == 0 and kk < 2) else ())
                    ms = mm(pb[bqk][:, 0:512], kfm[:, kk * 128:(kk + 1) * 128], qgrp,
                            True, True, waits=w0)
                    bank_written(bqk, ms)
                    pslc = pslb2[:, (kk % 2) * 512:(kk % 2) * 512 + 512]
                    te = so(lambda e, b=bqk, ps=pslc: e.activation(
                        ps, pb[b][:, 0:512], ACT.Exp, bias=t2[:, 0:1], scale=INV_SQRT_HD),
                        waits=(ms, tbias) + psl_free[kk % 2])
                    bank_read(bqk, te)
                    if kk >= nk - MASKN:
                        j = kk - (nk - MASKN)
                        te = vo(lambda e, ps=pslc, j_=j: e.tensor_tensor(
                            ps.rearrange("p (h t) -> p h t", h=4), ps.rearrange(
                                "p (h t) -> p h t", h=4),
                            masksb2[:, j_ * 128:(j_ + 1) * 128].rearrange(
                                "p (o t) -> p o t", o=1).to_broadcast([128, 4, 128]),
                            ALU.mult), waits=(te, tmk))
                    w6 = bank_start(6, (te,)) if kk == 0 else (te,)
                    msum_last = mm(pb[6][0:1, 0:512], onec[:, :], pslc, kk == 0, kk == nk - 1,
                                   waits=w6)
                    w7 = bank_start(7, (te,)) if kk == 0 else (te,)
                    my_last = mm(pb[7][:, 0:512], vshg[:, kk * 128:(kk + 1) * 128], pslc,
                                 kk == 0, kk == nk - 1, waits=w7)
                    psl_free[kk % 2] = (msum_last, my_last)
                bank_written(6, msum_last)
                bank_written(7, my_last)
                t_ = vo(lambda e: e.tensor_copy(rowbuf[0:1, 0:512], pb[6][0:1, 0:512]),
                        waits=(msum_last,) + tfin)
                bank_read(6, t_)
                t_ = vo(lambda e: e.reciprocal(rowbuf[0:1, 512:1024], rowbuf[0:1, 0:512]),
                        waits=(t_,))
                t_ = vo(lambda e: e.tensor_scalar(rowbuf[0:1, 512:1024], rowbuf[0:1, 512:1024],
                                                  0.5, None, ALU.mult), waits=(t_,))
                w5c = bank_start(5, (t_,))
                mrb = mm(pb[5][:, 0:512], onerow[:, :], rowbuf[0:1, 512:1024], True, True,
                         waits=w5c)
                bank_written(5, mrb)
                trb_ = vo(lambda e: e.tensor_copy(t1[:, 0:512], pb[5][:, 0:512]),
                          waits=(mrb,))
                bank_read(5, trb_)
                tev = vo(lambda e: e.tensor_tensor(t2[:, 512:1024], pb[7][:, 0:512],
                                                   t1[:, 0:512], ALU.mult),
                         waits=(my_last, trb_))
                bank_read(7, tev)
                xdst = xres.rearrange("p (c t) -> p c t", c=NCH)[
                    :, hg * 4:hg * 4 + 4, lt * 128:lt * 128 + 128]
                tev = vo(lambda e, x=xdst: e.tensor_tensor(
                    x, x, t2[:, 512:1024].rearrange("p (h t) -> p h t", h=4), ALU.add),
                    waits=(tev,))
                tfin = (tev,)
                prev_msk = psl_free[0] + psl_free[1]
            grp_guard = prev_msk + tfin
        p.barrier()

    def emit_l1_attn():
        first_acc = [True]
        ta_done = None
        txls = []
        tail5 = ()
        for it in range(NTT):
            tt0 = it * TT
            trs = ln_stats(tt0, waits=tail5)
            txls.append(center_tile(tt0, xlnA, True, waits=(trs,), dstride=TC, dbase=tt0))
            tail5 = (trs,)
        txl = tuple(txls)
        wt = {0: None, 1: None}
        for it in range(NTT):
            tt0 = it * TT
            for half in range(2):
                lt = it * 2 + half
                prev_rot = wt.get("rot", ())
                tcs = p.dma(SYN, cs_e[:, :], cos_ed[lt * 128:(lt + 1) * 128, :],
                            waits=prev_rot, sem="dcs")
                tcs = p.dma(SYN, cs_o[:, :], cos_od[lt * 128:(lt + 1) * 128, :],
                            waits=(tcs,), sem="dcs")
                tcs = p.dma(SYN, sn_e[:, :], sin_ed[lt * 128:(lt + 1) * 128, :],
                            waits=(tcs,), sem="dcs")
                tcs = p.dma(SYN, sn_o[:, :], sin_od[lt * 128:(lt + 1) * 128, :],
                            waits=(tcs,), sem="dcs")
                rot_toks = ()
                for part in range(2):
                    dst = ktok if part == 0 else vtok
                    tevs = []
                    for ob in range(16):
                        wbuf = wkv5a if ob % 2 == 0 else wkv5b
                        cbase = D + part * D + ob * 128
                        wl = p.dma(SYN, wbuf.rearrange("p (k j) -> p k j", j=128),
                                   kpj(winP[:, cbase:cbase + 128]),
                                   waits=(wt[ob % 2],) if wt[ob % 2] else (),
                                   sem="dk%d" % (ob % 2))
                        tgm = None
                        for kc in range(NCH):
                            tgm = vo(lambda e, w=wbuf, k=kc: e.tensor_scalar(
                                w[:, k * 128:(k + 1) * 128], w[:, k * 128:(k + 1) * 128],
                                l1gs[:, k:k + 1], None, ALU.mult),
                                waits=(wl,) if kc == 0 else (tgm,))
                        w4 = bank_start(4, (tgm,))
                        mbv = None
                        for kc in range(NCH):
                            mbv = mm(pb[4][0:1, 0:128], l1bs[:, kc:kc + 1],
                                     wbuf[:, kc * 128:(kc + 1) * 128], kc == 0, kc == NCH - 1,
                                     waits=w4 if kc == 0 else (tgm,))
                        bank_written(4, mbv)
                        tbr = p.dma(SYN, rowb2[0:1, 0:128], b_inP[0:1, cbase:cbase + 128],
                                    waits=(mbv,), sem="dbi")
                        tbv = vo(lambda e: e.tensor_copy(rowbuf[0:1, 0:128], pb[4][0:1, 0:128]),
                                 waits=(mbv,))
                        bank_read(4, tbv)
                        tbv = vo(lambda e: e.tensor_tensor(rowbuf[0:1, 0:128], rowbuf[0:1, 0:128],
                                                           rowb2[0:1, 0:128], ALU.add),
                                 waits=(tbv, tbr))
                        bk = ob % 2
                        mkv = None
                        for kc in range(NCH):
                            w0 = bank_start(bk, (tgm,) + txl) if kc == 0 else ()
                            mkv = mm(pb[bk][:, 0:128],
                                     xlnA[:, kc * TC + lt * 128:kc * TC + lt * 128 + 128],
                                     wbuf[:, kc * 128:(kc + 1) * 128], kc == 0, False,
                                     waits=w0 + (tgm,))
                        mkv = mm(pb[bk][:, 0:128], onerow[:, :], rowbuf[0:1, 0:128],
                                 False, True, waits=(tbv,))
                        bank_written(bk, mkv)
                        tev = vo(lambda e, b=bk, d=dst, o=ob: e.tensor_copy(
                            d[:, o * 128:(o + 1) * 128], pb[b][:, 0:128]), waits=(mkv,))
                        bank_read(bk, tev)
                        wt[ob % 2] = tev
                        tevs.append(tev)
                    tproj = tuple(tevs[-2:])
                    ke = dst[:, 0:512]; ko = dst[:, 512:1024]
                    t_ = vo(lambda e, a=ko: e.tensor_tensor(t1[:, 0:512], a, cs_o[:, :],
                                                            ALU.mult), waits=(tcs,) + tproj)
                    t_ = vo(lambda e, a=ke: e.tensor_tensor(t2[:, 0:512], a, sn_o[:, :],
                                                            ALU.mult), waits=(t_,))
                    t_ = vo(lambda e: e.tensor_tensor(t1[:, 0:512], t1[:, 0:512], t2[:, 0:512],
                                                      ALU.add), waits=(t_,))
                    t_ = vo(lambda e, a=ko: e.tensor_tensor(t2[:, 0:512], a, sn_e[:, :],
                                                            ALU.mult), waits=(t_,))
                    t_ = vo(lambda e, a=ke: e.tensor_tensor(a, a, cs_e[:, :], ALU.mult),
                            waits=(t_,))
                    t_ = vo(lambda e, a=ke: e.tensor_tensor(a, a, t2[:, 0:512], ALU.subtract),
                            waits=(t_,))
                    t_ = vo(lambda e, a=ko: e.tensor_copy(a, t1[:, 0:512]), waits=(t_,))
                    tn = so(lambda e, d=dst: e.activation(t1[:, 0:1024], d[:, 0:1024],
                                                          ACT.Square), waits=(t_,))
                    tn = vo(lambda e: e.tensor_reduce(t2[:, 0:1], t1[:, 0:1024], AXX, ALU.add),
                            waits=(tn,))
                    tn2 = so(lambda e, d=dst: e.activation(t1[:, 0:1024], d[:, 1024:2048],
                                                           ACT.Square), waits=(t_,))
                    tn2 = vo(lambda e: e.tensor_reduce(t2[:, 1:2], t1[:, 0:1024], AXX, ALU.add),
                             waits=(tn2,))
                    tn = vo(lambda e: e.tensor_tensor(t2[:, 0:1], t2[:, 0:1], t2[:, 1:2],
                                                      ALU.add), waits=(tn, tn2))
                    tn = so(lambda e: e.activation(t2[:, 0:1], t2[:, 0:1], ACT.Sqrt), waits=(tn,))
                    tn = vo(lambda e: e.reciprocal(t2[:, 2:3], t2[:, 0:1]), waits=(tn,))
                    tn = vo(lambda e, d=dst: e.tensor_scalar(d[:, :], d[:, :], t2[:, 2:3], None,
                                                             ALU.mult), waits=(tn,))
                    rot_toks = rot_toks + (tn, t_)
                wt["rot"] = rot_toks
                tpr = vo(lambda e: e.tensor_tensor(ktok[:, :], ktok[:, :], vtok[:, :], ALU.mult),
                         waits=rot_toks)
                for c in range(NCH):
                    w5 = bank_start(5, (tpr,)) if first_acc[0] else (tpr,)
                    ma = mm(pb[5][:, c:c + 1], ktok[:, c * 128:(c + 1) * 128], onec[:, :],
                            first_acc[0], (it == NTT - 1) and (half == 1) and (c == NCH - 1),
                            waits=w5)
                    first_acc[0] = False
                bank_written(5, ma)
                ta_done = ma
        tac = vo(lambda e: e.tensor_copy(acol[:, :], pb[5][:, 0:NH]), waits=(ta_done,))
        bank_read(5, tac)
        tac = p.dma(GPS, cca_in[:, :], acol[:, :], waits=(tac,))
        tcc = p.op(GPS, lambda e: e.collective_compute(
            "AllGather", ALU.bypass, replica_groups=RG,
            ins=[cca_in[:, :].opt()], outs=[cca_out[:, :].opt()]),
            waits=(tac,), inc="scc")
        ta0 = p.dma(GPS, agt0[:, :],
                    cca_out[0:1, :].rearrange("a (pp c) -> (a pp) c", pp=128), waits=(tcc,),
                    sem="da0")
        ta1 = p.dma(GPS, agt1[:, :],
                    cca_out[1:2, :].rearrange("a (pp c) -> (a pp) c", pp=128), waits=(tcc,),
                    sem="da1")
        tA = vo(lambda e: e.tensor_tensor(acol[:, :], agt0[:, :], agt1[:, :], ALU.add),
                waits=(ta0, ta1))
        tA = vo(lambda e: e.tensor_scalar(acol[:, :], acol[:, :], 0.5, 0.5, ALU.mult, ALU.add),
                waits=(tA,))
        wt2 = {0: None, 1: None}
        last_tq = ()
        for it in range(NTT):
            tt0 = it * TT
            tqa = tuple()
            for oc in range(NCH):
                wbuf = wkv5a if oc % 2 == 0 else wkv5b
                wl = p.dma(SYN, wbuf.rearrange("p (k j) -> p k j", j=128),
                           kpj(winP[:, oc * 128:(oc + 1) * 128]),
                           waits=(wt2[oc % 2],) if wt2[oc % 2] else (),
                           sem="dk%d" % (oc % 2))
                tgm = None
                for kc in range(NCH):
                    tgm = vo(lambda e, w=wbuf, k=kc: e.tensor_scalar(
                        w[:, k * 128:(k + 1) * 128], w[:, k * 128:(k + 1) * 128],
                        l1gs[:, k:k + 1], None, ALU.mult),
                        waits=(wl,) if kc == 0 else (tgm,))
                w4 = bank_start(4, (tgm,))
                mbv = None
                for kc in range(NCH):
                    mbv = mm(pb[4][0:1, 0:128], l1bs[:, kc:kc + 1],
                             wbuf[:, kc * 128:(kc + 1) * 128], kc == 0, kc == NCH - 1,
                             waits=w4 if kc == 0 else (tgm,))
                bank_written(4, mbv)
                tbr = p.dma(SYN, rowb2[0:1, 0:128], b_inP[0:1, oc * 128:(oc + 1) * 128],
                            waits=(mbv,), sem="dbi")
                tbv = vo(lambda e: e.tensor_copy(rowbuf[0:1, 0:128], pb[4][0:1, 0:128]),
                         waits=(mbv,))
                bank_read(4, tbv)
                tbv = vo(lambda e: e.tensor_tensor(rowbuf[0:1, 0:128], rowbuf[0:1, 0:128],
                                                   rowb2[0:1, 0:128], ALU.add), waits=(tbv, tbr))
                w3q = bank_start(3, (tbv,) + last_tq)
                ttr = tr(pb[3][:, 0:1], rowbuf[0:1, 0:128], waits=w3q)
                bank_written(3, ttr)
                tbc = vo(lambda e: e.tensor_copy(t2[:, 3:4], pb[3][:, 0:1]), waits=(ttr,))
                bank_read(3, tbc)
                bk = oc % 2
                mq = None
                for kc in range(NCH):
                    w0 = bank_start(bk, (tgm,)) if kc == 0 else ()
                    mq = mm(pb[bk][:, 0:TT], wbuf[:, kc * 128:(kc + 1) * 128],
                            xlnA[:, kc * TC + tt0:kc * TC + tt0 + TT], kc == 0, kc == NCH - 1,
                            waits=w0 + (tgm,) + txl)
                bank_written(bk, mq)
                tq_ = vo(lambda e, b=bk, o=oc: e.tensor_scalar(
                    xcen[:, o * TT:(o + 1) * TT], pb[b][:, 0:TT], t2[:, 3:4],
                    acol[:, o:o + 1], ALU.add, ALU.mult), waits=(mq, tA, tbc))
                bank_read(bk, tq_)
                last_tq = (tq_,)
                wt2[oc % 2] = tq_
                tqa = tqa + (tq_,)
            tqa = tqa[-2:]
            for oc in range(NCH):
                wbuf = wkv5a if oc % 2 == 0 else wkv5b
                wl = p.dma(SYN, wbuf.rearrange("p (k j) -> p k j", j=128),
                           kpj(woutP[:, oc * 128:(oc + 1) * 128]),
                           waits=(wt2[oc % 2],) if wt2[oc % 2] else (),
                           sem="dk%d" % (oc % 2))
                bk = oc % 2
                mo = None
                for kc in range(NCH):
                    w0 = bank_start(bk, (wl,) + tqa) if kc == 0 else ()
                    mo = mm(pb[bk][:, 0:TT], wbuf[:, kc * 128:(kc + 1) * 128],
                            xcen[:, kc * TT:(kc + 1) * TT], kc == 0, kc == NCH - 1,
                            waits=w0 + (wl,) + tqa)
                bank_written(bk, mo)
                xc = xres[:, oc * TC + tt0:oc * TC + tt0 + TT]
                to_ = vo(lambda e, x=xc, b=bk, o=oc: e.scalar_tensor_tensor(
                    x, pb[b][:, 0:TT], bouts[:, o:o + 1], x, ALU.add, ALU.add), waits=(mo,))
                bank_read(bk, to_)
                to_ = vo(lambda e, x=xc, o=oc, t0=tt0: e.tensor_tensor(
                    x, x, xlnA[:, o * TC + t0:o * TC + t0 + TT], ALU.add), waits=(to_,))
                wt2[oc % 2] = to_
            p.barrier()

    if "p1" in phases:
        emit_l0_ffn()
    if "p2" in phases:
        emit_l0_attn()
    if "p3" in phases:
        emit_l0_ffn()
    if "p4" in phases:
        emit_l1_ffn()
    if "p5" in phases:
        emit_l1_attn()
    if "p6" in phases:
        emit_l1_ffn()

    outs = []
    for c in range(NCH):
        outs.append(p.dma(GPS, x_out[c * 128:(c + 1) * 128, :],
                          xres[:, c * TC:(c + 1) * TC]))
    p.wait(GPS, outs[-1])
    p.replay()
    es.close()
    return nc



def stage_inputs(inputs):
    f = np.float32
    W = {k: np.asarray(v) for k, v in inputs.items()}
    perm = np.concatenate([np.arange(0, 1024, 2), np.arange(1, 1024, 2),
                           np.arange(1024, 2048)])
    shared = {
        "wqT": np.ascontiguousarray(W["L0_attn_Wq"].T.astype(f)),
        "wvT": np.ascontiguousarray(W["L0_attn_Wv"].T.astype(f)),
        "wgT": np.ascontiguousarray(W["L0_attn_Wg"].T.astype(f)),
        "woT": np.ascontiguousarray(W["L0_attn_Wo"].T.astype(f)),
        "l0g": np.ascontiguousarray(W["L0_attn_ln_gamma"].reshape(NCH, 128).T.astype(f)),
        "l0b": np.ascontiguousarray(W["L0_attn_ln_beta"].reshape(NCH, 128).T.astype(f)),
        "beta_x": np.ascontiguousarray(np.repeat(W["L0_attn_beta"][None, :], 128, 0).astype(f)),
        "beta_row": np.ascontiguousarray(W["L0_attn_beta"][None, :].astype(f)),
        "wgf": np.ascontiguousarray(W["L0_ffn_w_gate"].T.astype(f)),
        "wuf": np.ascontiguousarray(W["L0_ffn_w_up"].T.astype(f)),
        "wdf": np.ascontiguousarray(W["L0_ffn_w_down"].T.astype(f)),
        "winP": None,
        "b_inP": None,
        "b_out_c": np.ascontiguousarray(W["L1_attn_b_out"].reshape(NCH, 128).T.astype(f)),
        "l1g": np.ascontiguousarray(W["L1_attn_gamma"].reshape(NCH, 128).T.astype(f)),
        "l1b": np.ascontiguousarray(W["L1_attn_beta"].reshape(NCH, 128).T.astype(f)),
        "wgateT": np.ascontiguousarray(W["L1_ffn_W_gate"].T.astype(f)),
        "wqkvT": np.ascontiguousarray(W["L1_ffn_W_qkv"].T.astype(f)),
        "woutP": np.ascontiguousarray(W["L1_attn_W_out"][perm, :].astype(f)),
    }
    perm3 = np.concatenate([perm, D + perm, 2 * D + perm])
    shared["winP"] = np.ascontiguousarray(W["L1_attn_W_in"][:, perm3].astype(f))
    shared["b_inP"] = np.ascontiguousarray(W["L1_attn_b_in"][perm3][None, :].astype(f))
    x = np.asarray(W["L0_attn_x"], dtype=f)
    maps = []
    for core in range(8):
        b, h = core // 2, core % 2
        toks = [tok_global(h, l) for l in range(TC)]
        m = dict(shared)
        m["x_in"] = np.ascontiguousarray(x[b][toks, :].T)
        m.update(host_constants(h))
        maps.append(m)
    return maps


def kernel(**inputs):
    nc = build_program(ALL_PHASES)
    maps = stage_inputs(inputs)
    res = run_bass_kernel_spmd(nc, maps, core_ids=list(range(8)))
    out = np.empty((B, S, D), dtype=np.float32)
    for core in range(8):
        b, h = core // 2, core % 2
        toks = [tok_global(h, l) for l in range(TC)]
        out[b][toks, :] = res.results[core]["x_out"].T
    return out
